# version 28 (final)
# baseline (speedup 1.0000x reference)
import numpy as np
import ml_dtypes
import concourse.bass as bass
import concourse.mybir as mybir
from concourse.bass_utils import run_bass_kernel_spmd

F32 = mybir.dt.float32
F16 = mybir.dt.float16
F8 = mybir.dt.float8e4
AF = mybir.ActivationFunctionType
ALU = mybir.AluOpType
AX = mybir.AxisListType.X

B, S, D = 4, 4096, 1024
NHEAD, HD = 16, 64
T, C, TT, NTT = 2048, 8, 512, 4
HG, HPG = 4, 4
EPS = 1e-5
RG = [[0, 1], [2, 3], [4, 5], [6, 7]]


class Sem:
    def __init__(self, nc, name):
        self.h = nc.alloc_semaphore(name)
        self.n = 0


class EQ:
    def __init__(self):
        self.items = []

    def wait(self, sem, val):
        if val > 0:
            self.items.append(("w", sem.h, val))

    def run(self, fn, inc=None, amt=1):
        tgt = None
        if inc is not None:
            inc.n += amt
            tgt = (inc.h, inc.n)
        self.items.append(("r", fn, tgt))
        return inc.n if inc is not None else None


def build_nc(dump=None, audit=False):
    nc = bass.Bass("TRN2", target_bir_lowering=False)

    def din(name, shape, dtype=F32):
        return nc.dram_tensor(name, shape, dtype, kind="ExternalInput").ap()

    x0_d = din("x0", [D, T])
    wqkv_d = din("wqkv", [HG, C, 128, 768])
    qkvbc_d = din("qkvbc", [128, HG * 6])
    wout_d = din("wout", [C, 128, D])
    outbc_d = din("outbc", [128, C])
    fker_d = din("fker", [C, 128, D])
    fbias_d = din("fbias", [1, D])
    fa_d = din("fa", [C, 128, 16])
    fb_d = din("fb", [16, D])
    cones_d = din("cones", [1, T])
    mask_d = din("maskio", [128, 256])
    idf_d = din("idf", [128, 128])

    y_d = nc.dram_tensor("y", [D, T], F32, kind="ExternalOutput").ap()

    agink = [nc.dram_tensor(f"agink{g}", [256, 2 * T], F16, kind="Internal") for g in range(HG)]
    aginv = [nc.dram_tensor(f"aginv{g}", [HPG * T, HD], F32, kind="Internal") for g in range(HG)]
    kvk = [nc.dram_tensor(f"kvk{g}", [512, 2 * T], F16, kind="Internal") for g in range(HG)]
    kvv = [nc.dram_tensor(f"kvv{g}", [2 * HPG * T, HD], F32, kind="Internal") for g in range(HG)]
    qsp = [nc.dram_tensor(f"qsp{g}", [256, 2 * T], F16, kind="Internal") for g in range(HG)]
    x1sp = y_d if dump == "x1" else nc.dram_tensor("x1sp", [D, T], F32, kind="Internal").ap()
    x2sp = y_d if dump == "x2" else nc.dram_tensor("x2sp", [D, T], F32, kind="Internal").ap()
    scr128 = nc.dram_tensor("scr128", [128], F32, kind="Internal")
    otd = nc.dram_tensor("otd", [D, T], F32, kind="Internal")
    qd = nc.dram_tensor("qd", [128, C * D], F16, kind="Internal")

    A = nc.alloc_sbuf_tensor
    af = A("af", [128, 128], F32).ap()
    ahi = A("ahi", [128, 128], F16).ap()
    alo = A("alo", [128, 128], F16).ap()
    beff = A("beff", [128, D], F32).ap()
    uo = A("uo", [128, T], F32).ap()
    qbias = A("qbias", [128, HG * 6], F32).ap()
    obias = A("obias", [128, C], F32).ap()
    maskt = A("maskt", [128, 256], F32).ap()
    idf = A("idfs", [128, 128], F32).ap()
    onesf = A("onesfp", [128, 512], F32).ap()
    bc1 = A("bc1", [128, 128], F32).ap()
    bcst = A("bcst", [128, 8], F32).ap()
    smalls = A("smalls", [1, 1152], F32).ap()
    gmax8 = A("gmax8", [128, 16], F32).ap()
    rowv = A("rowv", [1, 128], F32).ap()

    AB = (nc.sbuf_base + 31) & ~31
    ARENA = nc.sbuf_top - AB

    def at(name, shape, dtype, off):
        esz = {F16: 2, F8: 1}.get(dtype, 4)
        end = off + shape[1] * esz
        assert end <= ARENA, f"{name}: arena overflow {end} > {ARENA}"
        return nc.alloc_sbuf_tensor_at(name, shape, dtype, offset=AB + off).ap()

    xs = at("xs", [128, C * T], F32, 0)
    x1f = at("x1f", [128, C * T], F32, 0)
    qw = at("qw", [128, C * D], F32, 65536)
    qw16 = at("qw16", [128, C * D], F16, 65536)
    ln_sq = at("ln_sq", [128, 4096], F32, 98304)
    ln_st = at("ln_st", [128, 4 * 512], F32, 114688)
    xhi = at("xhi", [128, C * T], F16, 122880)
    xlo = at("xlo", [128, C * T], F16, 155648)
    qsc = at("qsc", [128, 4096], F32, 98304)
    q8a = at("q8a", [128, 1024], F8, 110592)
    q8b = at("q8b", [128, 1024], F8, 111616)
    wstag = at("wstag", [128, 2 * C * 768], F32, 65536)
    spst = at("spst", [128, 2 * T], F32, 114688)
    vtr2 = at("vtr2", [128, 2 * T], F32, 131072)
    sp16 = at("sp16", [128, 2 * 2 * T], F16, 147456)
    kthi = at("kthi", [128, 2 * S], F16, 65536)
    ktlo = at("ktlo", [128, 2 * S], F16, 81920)
    qthi = at("qthi", [128, 2 * T], F16, 98304)
    qtlo = at("qtlo", [128, 2 * T], F16, 106496)
    vtile = at("vtile", [128, 2 * 32 * 65], F32, 114688)
    ptile = at("ptile", [128, 2 * 2 * 512], F32, 131392)
    otst = at("otst", [128, 2 * 512], F32, 139712)
    rlst = at("rlst", [128, 2 * 512], F32, 143808)
    otf = at("otf", [128, C * T], F32, 65536)
    wout_s = at("wout_s", [128, C * D], F32, 131072)

    ps = nc.alloc_psum_tensor("ps", [128, 4096], F32).ap()

    sLD, sCC = Sem(nc, "sLD"), Sem(nc, "sCC")
    sGB, sX1, sX2, sQW = Sem(nc, "sGB"), Sem(nc, "sX1"), Sem(nc, "sX2"), Sem(nc, "sQW")
    sKQ = [Sem(nc, "sKQ0"), Sem(nc, "sKQ1")]
    sVD = [Sem(nc, "sVD0"), Sem(nc, "sVD1")]
    sWQ = [Sem(nc, "sWQ0"), Sem(nc, "sWQ1")]
    sK = [Sem(nc, "sK0"), Sem(nc, "sK1")]
    sQL = [Sem(nc, "sQL0"), Sem(nc, "sQL1")]
    sVL = [Sem(nc, "sVL0"), Sem(nc, "sVL1")]
    sX1L, sOTL, sQWL = Sem(nc, "sX1L"), Sem(nc, "sOTL"), Sem(nc, "sQWL")
    sGL = Sem(nc, "sGL")
    sXT = [Sem(nc, f"sXT{i}") for i in range(NTT)]
    sOT = [Sem(nc, "sOT0"), Sem(nc, "sOT1")]
    sX2L = Sem(nc, "sX2L")
    sT, sV, sS = Sem(nc, "sT"), Sem(nc, "sV"), Sem(nc, "sS")
    sOUT = Sem(nc, "sOUT")
    SY, GP, VE, SC, TE = EQ(), EQ(), EQ(), EQ(), EQ()

    ld = {}

    def sydma(key, out, in_, sem=sLD):
        SY.run(lambda e, out=out, in_=in_: e.dma_start(out=out, in_=in_), inc=sem, amt=16)
        ld[key] = sem.n

    xs_c = xs.rearrange("p (c t) -> p c t", t=T)
    x0_c = x0_d.rearrange("(c p) t -> p c t", p=128)
    ld_t = []
    for tt in range(NTT):
        q_ = SY if tt < 2 else GP
        sm = sXT[tt]
        q_.run(lambda e, tt=tt: e.dma_start(out=xs_c[:, :, tt * TT:(tt + 1) * TT],
                                            in_=x0_c[:, :, tt * TT:(tt + 1) * TT]), inc=sm, amt=16)
        ld_t.append((sm, sm.n))
    GP.run(lambda e: e.dma_start(out=qw.rearrange("p (c t) -> p c t", t=D),
                                 in_=fker_d.rearrange("c p t -> p c t")), inc=sGL, amt=16)
    ld_fk = sGL.n
    sydma("mask", maskt, mask_d)
    sydma("idf", idf, idf_d)
    sydma("fa", af.rearrange("p (c t) -> p c t", t=16), fa_d.rearrange("c p t -> p c t"))
    SY.wait(sV, 5)
    sydma("fb", beff[0:16, :], fb_d)
    sydma("fbias", beff[16:17, :], fbias_d)
    sydma("cones", uo[16:17, :], cones_d)
    sydma("qbias", qbias, qkvbc_d)
    sydma("obias", obias, outbc_d)
    ld_b1 = sLD.n

    VE.run(lambda e: e.memset(onesf, 1.0), inc=sV)
    VE.run(lambda e: e.memset(beff, 0.0), inc=sV)
    VE.run(lambda e: e.memset(uo, 0.0), inc=sV)
    VE.run(lambda e: e.memset(bc1, 0.0), inc=sV)
    v_z0 = VE.run(lambda e: e.memset(bc1[0:1, 0:64], 1.0), inc=sV)

    st8 = {
        "ln_sq_free": (sV, 0), "v_af": 0,
        "evac": {b: None for b in range(8)},
        "u_rd": 0,
    }

    def wev(q, bank):
        m = st8["evac"][bank]
        if m is not None:
            q.wait(m[0], m[1])

    def plan_quant():
        q_t, q_m = qsc[:, 0:1024], qsc[:, 1024:2048]
        v_qc = []
        for c in range(C):
            wc = qw[:, c * D:(c + 1) * D]
            VE.run(lambda e, wc=wc: e.tensor_scalar(q_t, wc, RS, None, ALU.mult), inc=sV)
            VE.run(lambda e: e.tensor_scalar(q8a, q_t, 0.0, 8.0, ALU.max, ALU.add), inc=sV)
            VE.run(lambda e: e.tensor_scalar(q_m, q_t, 0.0, -1.0, ALU.min, ALU.mult), inc=sV)
            VE.run(lambda e: e.tensor_single_scalar(q8b, q_m, 8.0, ALU.add), inc=sV)
            v_qc.append((sV, VE.run(lambda e, c=c: e.tensor_tensor(
                qw16[:, c * D:(c + 1) * D], q8a, q8b, ALU.subtract), inc=sV)))
        GP.wait(sV, v_qc[-1][1])
        GP.run(lambda e: e.dma_start(out=qd.ap(), in_=qw16), inc=sQW, amt=16)
        return v_qc

    def plan_ln_phase(src, dst, pre_wait=None, gp_wait=None, tile_pre=None):
        sq = ln_sq
        t_stats = [0] * NTT
        for tt in range(NTT):
            if tt == 0:
                for w in (pre_wait or []):
                    SC.wait(*w)
            if tile_pre is not None:
                SC.wait(*tile_pre[tt])
            SC.wait(*st8["ln_sq_free"])
            s_sq = 0
            for c in range(C):
                xc = src[:, c * T + tt * TT: c * T + tt * TT + TT]
                s_sq = SC.run(lambda e, xc=xc, c=c: e.activation(
                    sq[:, c * 512:(c + 1) * 512], xc, AF.Square), inc=sS)
            if tt == 0:
                for w in (pre_wait or []):
                    TE.wait(*w)
            if tile_pre is not None:
                TE.wait(*tile_pre[tt])
            wev(TE, 2 * tt)
            wev(TE, 2 * tt + 1)
            TE.wait(sS, s_sq)
            for c in range(C):
                xc = src[:, c * T + tt * TT: c * T + tt * TT + TT]
                st = (c == 0)
                last = (c == C - 1)
                TE.run(lambda e, xc=xc, st=st, last=last, tt=tt: e.matmul(
                    ps[:, tt * 1024: tt * 1024 + 512], onesf[:, 0:128], xc, start=st, stop=last))
                TE.run(lambda e, c=c, st=st, last=last, tt=tt: e.matmul(
                    ps[:, tt * 1024 + 512:(tt + 1) * 1024], onesf[:, 0:128], sq[:, c * 512:(c + 1) * 512],
                    start=st, stop=last), inc=sT)
            t_stats[tt] = sT.n
            st8["ln_sq_free"] = (sT, t_stats[tt])
        v_apply = (sV, 0)
        tile_apply = []
        for tt in range(NTT):
            m_, ve_, r_, b_ = (ln_st[:, 0:512], ln_st[:, 512:1024], ln_st[:, 1024:1536], ln_st[:, 1536:2048])
            VE.wait(sT, t_stats[tt])
            VE.run(lambda e, tt=tt: e.tensor_single_scalar(m_, ps[:, tt * 1024: tt * 1024 + 512],
                                                           1.0 / D, ALU.mult), inc=sV)
            VE.run(lambda e, tt=tt: e.tensor_single_scalar(ve_, ps[:, tt * 1024 + 512:(tt + 1) * 1024],
                                                           1.0 / D, ALU.mult), inc=sV)
            st8["evac"][2 * tt] = (sV, sV.n)
            st8["evac"][2 * tt + 1] = (sV, sV.n)
            VE.run(lambda e: e.tensor_tensor(r_, m_, m_, ALU.mult), inc=sV)
            VE.run(lambda e: e.tensor_tensor(ve_, ve_, r_, ALU.subtract), inc=sV)
            VE.run(lambda e: e.tensor_single_scalar(ve_, ve_, EPS, ALU.add), inc=sV)
            v_rc = VE.run(lambda e: e.reciprocal(ve_, ve_), inc=sV)
            SC.wait(sV, v_rc)
            s_r = SC.run(lambda e: e.activation(r_, ve_, AF.Sqrt), inc=sS)
            VE.wait(sS, s_r)
            VE.run(lambda e: e.scalar_tensor_tensor(b_, m_, -1.0, r_, ALU.mult, ALU.mult), inc=sV)
            if gp_wait is not None and tt == 0:
                VE.wait(*gp_wait)
            if dst is not None:
                for c in range(C):
                    xc = src[:, c * T + tt * TT: c * T + tt * TT + TT]
                    dc = dst[:, c * T + tt * TT: c * T + tt * TT + TT]
                    VE.run(lambda e, dc=dc, xc=xc: e.tensor_tensor(dc, xc, r_, ALU.mult), inc=sV)
                    va = VE.run(lambda e, dc=dc: e.tensor_tensor(dc, dc, b_, ALU.add), inc=sV)
                v_apply = (sV, va)
                tile_apply.append(v_apply)
                continue
            slots = (m_, ve_)
            s_cast = {}
            for c in range(C):
                xc = src[:, c * T + tt * TT: c * T + tt * TT + TT]
                hc = xhi[:, c * T + tt * TT: c * T + tt * TT + TT]
                lc = xlo[:, c * T + tt * TT: c * T + tt * TT + TT]
                sl = slots[c % 2]
                VE.run(lambda e, sl=sl, xc=xc: e.tensor_tensor(sl, xc, r_, ALU.mult), inc=sV)
                v_st = VE.run(lambda e, sl=sl: e.tensor_tensor(sl, sl, b_, ALU.add), inc=sV)
                SC.wait(sV, v_st)
                s_cast[c] = SC.run(lambda e, hc=hc, sl=sl: e.activation(hc, sl, AF.Copy), inc=sS)
                if c >= 1:
                    VE.wait(sS, s_cast[c - 1])
                    pc = c - 1
                    hp = xhi[:, pc * T + tt * TT: pc * T + tt * TT + TT]
                    lp = xlo[:, pc * T + tt * TT: pc * T + tt * TT + TT]
                    va = VE.run(lambda e, lp=lp, hp=hp, sl=slots[pc % 2]: e.tensor_tensor(
                        lp, sl, hp, ALU.subtract), inc=sV)
            VE.wait(sS, s_cast[C - 1])
            lce = xlo[:, (C - 1) * T + tt * TT: (C - 1) * T + tt * TT + TT]
            hce = xhi[:, (C - 1) * T + tt * TT: (C - 1) * T + tt * TT + TT]
            va = VE.run(lambda e, sl=slots[(C - 1) % 2], lce=lce, hce=hce: e.tensor_tensor(
                lce, sl, hce, ALU.subtract), inc=sV)
            v_apply = (sV, va)
            tile_apply.append(v_apply)
        return v_apply, tile_apply, t_stats[NTT - 1]

    def plan_ffn_all(evac_fn, v_xln, tiles, qwait=None):
        for b in (4, 5, 6, 7):
            wev(TE, b)
        TE.wait(sV, st8["v_af"])
        for tt in range(NTT):
            TE.wait(*tiles[tt])
            for c in range(C):
                for pi, (am, xm) in enumerate(((ahi, xhi), (ahi, xlo), (alo, xhi))):
                    ac = am[:, c * 16:(c + 1) * 16]
                    st = (c == 0 and pi == 0)
                    last = (c == C - 1 and pi == 2)
                    xc = xm[:, c * T + tt * TT: c * T + tt * TT + TT]
                    TE.run(lambda e, ac=ac, xc=xc, st=st, tt=tt, last=last: e.matmul(
                        ps[0:16, 2048 + tt * 512: 2048 + (tt + 1) * 512], ac, xc, start=st,
                        stop=last), inc=(sT if (last and tt == NTT - 1) else None))
        TE.wait(*v_xln)
        t_u = sT.n
        SC.wait(sT, t_u)
        SC.wait(sT, st8["u_rd"])
        v_u = SC.run(lambda e: e.activation(uo[0:16, :], ps[0:16, 2048:4096], AF.Copy), inc=sS)
        for b in (4, 5, 6, 7):
            st8["evac"][b] = (sS, v_u)
        for mc in range(C):
            bs = (mc % 2) * 4
            banks = [bs, bs + 1, bs + 2, bs + 3]
            for tt in range(NTT):
                wev(TE, banks[tt])
            if mc == 0:
                TE.wait(sS, v_u)
            bv = beff[:, mc * 128:(mc + 1) * 128]
            for tt in range(NTT):
                TE.run(lambda e, bv=bv, tt=tt, b=banks[tt]: e.matmul(
                    ps[:, b * 512:(b + 1) * 512], bv, uo[:, tt * 512:(tt + 1) * 512],
                    start=True, stop=False))
            marks = {}
            for c in range(C):
                if qwait is not None and mc == 0:
                    TE.wait(*qwait[c])
                qc = qw16[:, c * D + mc * 128: c * D + (mc + 1) * 128]
                for tt in range(NTT):
                    for pi, xm in enumerate((xhi, xlo)):
                        lastmm = (c == C - 1 and pi == 1)
                        xc = xm[:, c * T + tt * TT: c * T + tt * TT + TT]
                        TE.run(lambda e, qc=qc, xc=xc, b=banks[tt], lastmm=lastmm: e.matmul(
                            ps[:, b * 512:(b + 1) * 512], qc, xc, start=False, stop=lastmm),
                            inc=(sT if lastmm else None))
                        if lastmm:
                            marks[tt] = sT.n
            for tt in range(NTT):
                evac_fn(mc, tt, ps[:, banks[tt] * 512:(banks[tt] + 1) * 512], marks[tt], banks[tt])
        st8["u_rd"] = sT.n

    v_ln, ta_ln1, _ = plan_ln_phase(xs, None, pre_wait=[(sV, 5)], tile_pre=ld_t)
    VE.wait(sGL, ld_fk)
    VE.wait(sLD, ld_b1)
    for c in range(C):
        VE.run(lambda e, c=c: e.reduce_max(gmax8[:, c:c + 1], qw[:, c * D:(c + 1) * D], AX,
                                           apply_absolute_value=True), inc=sV)
    v_gmax = VE.run(lambda e: e.reduce_max(gmax8[:, 8:9], gmax8[:, 0:8], AX), inc=sV)
    GP.wait(sV, v_gmax)
    GP.run(lambda e: e.dma_start(out=scr128.ap().rearrange("(a b) -> a b", a=128), in_=gmax8[:, 8:9]),
           inc=sGB, amt=16)
    GP.wait(sGB, sGB.n)
    GP.run(lambda e: e.dma_start(out=rowv, in_=scr128.ap().rearrange("(a b) -> a b", a=1)), inc=sGB, amt=16)
    VE.wait(sGB, sGB.n)
    VE.run(lambda e: e.reduce_max(smalls[0:1, 1040:1041], rowv, AX), inc=sV)
    VE.run(lambda e: e.reciprocal(smalls[0:1, 1041:1042], smalls[0:1, 1040:1041]), inc=sV)
    VE.run(lambda e: e.tensor_single_scalar(smalls[0:1, 1042:1043], smalls[0:1, 1041:1042], 7.0, ALU.mult), inc=sV)
    v_rs = VE.run(lambda e: e.tensor_single_scalar(smalls[0:1, 1043:1044], smalls[0:1, 1040:1041],
                                                   1.0 / 14.0, ALU.mult), inc=sV)
    TE.wait(sV, v_rs)
    t_bc = TE.run(lambda e: e.matmul(ps[:, 0:2], onesf[0:1, 0:128], smalls[0:1, 1042:1044],
                                     start=True, stop=True), inc=sT)
    VE.wait(sT, t_bc)
    v_bc = VE.run(lambda e: e.tensor_copy(bcst[:, 0:2], ps[:, 0:2]), inc=sV)
    RS = bcst[:, 0:1]
    HS = bcst[:, 1:2]
    st8["evac"][0] = (sV, v_bc)

    VE.run(lambda e: e.tensor_scalar(beff[0:17, :], beff[0:17, :], bcst[0:17, 0:1], None, ALU.mult), inc=sV)
    VE.run(lambda e: e.tensor_copy(ahi, af), inc=sV)
    VE.run(lambda e: e.tensor_copy(ln_st[:, 0:128], ahi), inc=sV)
    v_af = VE.run(lambda e: e.tensor_tensor(alo, af, ln_st[:, 0:128], ALU.subtract), inc=sV)
    st8["v_af"] = v_af

    v_qc = plan_quant()

    g_x1 = {}

    def ffn1_evac(mc, tt, pb, t_mm, bank):
        VE.wait(sT, t_mm)
        dst = xs[:, mc * T + tt * TT: mc * T + tt * TT + TT]
        VE.run(lambda e, pb=pb, dst=dst: e.scalar_tensor_tensor(dst, pb, HS, dst, ALU.mult, ALU.add), inc=sV)
        st8["evac"][bank] = (sV, sV.n)

    plan_ffn_all(ffn1_evac, v_ln, ta_ln1, qwait=v_qc)
    v_x1 = sV.n
    GP.wait(sV, v_x1)
    GP.run(lambda e: e.dma_start(out=x1sp.rearrange("(c p) t -> p c t", p=128),
                                 in_=xs.rearrange("p (c t) -> p c t", t=T)), inc=sX1, amt=16)

    if dump == "x1":
        GP.wait(sX1, 16)
        flush(nc, SY, GP, VE, SC, TE, audit)
        return nc

    v_xln2, ta_ln2, t_ln2sums = plan_ln_phase(xs, xs, pre_wait=[(sV, sV.n)], gp_wait=(sX1, 16))

    wq_rd = {0: 0, 1: 0}
    sp_free = [(None, 0), (None, 0)]
    sp16_free = [(None, 0), (None, 0)]
    vt_free = [(None, 0), (None, 0)]
    for hg in range(HG):
        wb_ = hg % 2
        if hg == 0:
            SY.wait(sV, v_qc[-1][1])
            SY.wait(sT, sT.n)
            SY.wait(sS, sS.n)
            SY.wait(sQW, 16)
        SY.wait(sT, wq_rd[wb_])
        sydma(f"wq{hg}", wstag[:, wb_ * 6144:(wb_ + 1) * 6144].rearrange("p (c t) -> p c t", t=768),
              wqkv_d[hg].rearrange("c p t -> p c t"), sem=sWQ[wb_])
        TE.wait(sWQ[wb_], ld[f"wq{hg}"])
        for mc in range(6):
            sb = mc % 2
            bs = sb * 4
            banks = [bs, bs + 1, bs + 2, bs + 3]
            if hg == 0 and mc == 0:
                for tt in range(NTT):
                    wev(TE, banks[tt])
                    TE.wait(*ta_ln2[tt])
                    for c in range(C):
                        wc = wstag[:, wb_ * 6144 + c * 768 + mc * 128: wb_ * 6144 + c * 768 + (mc + 1) * 128]
                        xc = xs[:, c * T + tt * TT: c * T + tt * TT + TT]
                        TE.run(lambda e, wc=wc, xc=xc, st=(c == 0), b=banks[tt], lastc=(c == C - 1): e.matmul(
                            ps[:, b * 512:(b + 1) * 512], wc, xc, start=st, stop=lastc),
                            inc=(sT if (c == C - 1 and tt == NTT - 1) else None))
            else:
                for tt in range(NTT):
                    wev(TE, banks[tt])
                for c in range(C):
                    wc = wstag[:, wb_ * 6144 + c * 768 + mc * 128: wb_ * 6144 + c * 768 + (mc + 1) * 128]
                    for tt in range(NTT):
                        st = (c == 0)
                        xc = xs[:, c * T + tt * TT: c * T + tt * TT + TT]
                        TE.run(lambda e, wc=wc, xc=xc, st=st, b=banks[tt], lastc=(c == C - 1): e.matmul(
                            ps[:, b * 512:(b + 1) * 512], wc, xc, start=st, stop=lastc),
                            inc=(sT if (c == C - 1 and tt == NTT - 1) else None))
            t_mm = sT.n
            SC.wait(sT, t_mm)
            if sp_free[sb][0] is not None:
                SC.wait(*sp_free[sb])
            qb_col = qbias[:, hg * 6 + mc: hg * 6 + mc + 1]
            s_sp = 0
            for tt in range(NTT):
                s_sp = SC.run(lambda e, sb=sb, tt=tt, b=banks[tt], qb_col=qb_col: e.activation(
                    spst[:, sb * T + tt * 512: sb * T + (tt + 1) * 512],
                    ps[:, b * 512:(b + 1) * 512], AF.Identity, bias=qb_col), inc=sS)
                st8["evac"][banks[tt]] = (sS, s_sp)
            if mc < 2 or mc >= 4:
                VE.wait(sS, s_sp)
                if sp16_free[sb][0] is not None:
                    VE.wait(*sp16_free[sb])
                v_hl = 0
                for tt in range(NTT):
                    sv = spst[:, sb * T + tt * 512: sb * T + (tt + 1) * 512]
                    hv = sp16[:, sb * 2 * T + tt * 512: sb * 2 * T + (tt + 1) * 512]
                    lv = sp16[:, sb * 2 * T + T + tt * 512: sb * 2 * T + T + (tt + 1) * 512]
                    VE.run(lambda e, hv=hv, sv=sv: e.tensor_copy(hv, sv), inc=sV)
                    v_hl = VE.run(lambda e, lv=lv, sv=sv, hv=hv: e.tensor_tensor(
                        lv, sv, hv, ALU.subtract), inc=sV)
                sp_free[sb] = (sV, v_hl)
                GP.wait(sV, v_hl)
                dst = (agink[hg].ap()[mc * 128:(mc + 1) * 128, :] if mc < 2
                       else qsp[hg].ap()[(mc - 4) * 128:(mc - 3) * 128, :])
                GP.run(lambda e, dst=dst, sb=sb: e.dma_start(
                    out=dst, in_=sp16[:, sb * 2 * T:(sb + 1) * 2 * T]), inc=sKQ[sb], amt=16)
                sp16_free[sb] = (sKQ[sb], sKQ[sb].n)
            else:
                mv = mc - 2
                if vt_free[mv][0] is not None:
                    VE.wait(*vt_free[mv])
                v_tr = 0
                for blk in range(16):
                    tb = 6 + (blk % 2)
                    pbt = ps[:, tb * 512: tb * 512 + 128]
                    wev(TE, tb)
                    TE.wait(sS, s_sp)
                    t_tr = TE.run(lambda e, sb=sb, blk=blk, pbt=pbt: e.transpose(
                        pbt, spst[:, sb * T + blk * 128: sb * T + (blk + 1) * 128],
                        identity=idf), inc=sT)
                    VE.wait(sT, t_tr)
                    for hh in range(2):
                        VE.run(lambda e, mv=mv, hh=hh, blk=blk, pbt=pbt: e.tensor_copy(
                            vtr2[:, mv * T + hh * 1024 + blk * 64: mv * T + hh * 1024 + (blk + 1) * 64],
                            pbt[:, hh * 64:(hh + 1) * 64]), inc=sV)
                    v_tr = sV.n
                    st8["evac"][tb] = (sV, v_tr)
                GP.wait(sV, v_tr)
                for hh in range(2):
                    hloc = mv * 2 + hh
                    GP.run(lambda e, hg=hg, hloc=hloc, mv=mv, hh=hh: e.dma_start(
                        out=aginv[hg].ap().rearrange("(h j p) f -> h p j f", j=16, p=128)[hloc],
                        in_=vtr2[:, mv * T + hh * 1024: mv * T + (hh + 1) * 1024]
                            .rearrange("p (j f) -> p j f", f=64)), inc=sVD[mv], amt=16)
                vt_free[mv] = (sVD[mv], sVD[mv].n)
                sp_free[sb] = (sT, sT.n)
        wq_rd[wb_] = sT.n
        if hg >= 1:
            GP.wait(sCC, 2 * hg)
        GP.wait(sKQ[0], sKQ[0].n)
        GP.wait(sKQ[1], sKQ[1].n)
        GP.wait(sVD[0], sVD[0].n)
        GP.wait(sVD[1], sVD[1].n)
        GP.run(lambda e, hg=hg: e.collective_compute(
            "AllGather", ALU.bypass, replica_groups=RG,
            ins=[agink[hg].ap().opt()], outs=[kvk[hg].ap().opt()]), inc=sCC)
        GP.run(lambda e, hg=hg: e.collective_compute(
            "AllGather", ALU.bypass, replica_groups=RG,
            ins=[aginv[hg].ap().opt()], outs=[kvv[hg].ap().opt()]), inc=sCC)

    SC.wait(sVD[0], sVD[0].n)
    SC.wait(sVD[1], sVD[1].n)
    VE.wait(sKQ[0], sKQ[0].n)
    VE.wait(sKQ[1], sKQ[1].n)
    VE.wait(sVD[0], sVD[0].n)
    VE.wait(sVD[1], sVD[1].n)
    VE.wait(sT, sT.n)
    for buf in range(2):
        VE.run(lambda e, buf=buf: e.memset(
            vtile[:, buf * 2080:(buf + 1) * 2080].rearrange("p (s f) -> p s f", f=65)[:, :, 64:65], 1.0), inc=sV)
    VE.run(lambda e: e.memset(kthi[64:128, :], 0.0), inc=sV)
    VE.run(lambda e: e.memset(ktlo[64:128, :], 0.0), inc=sV)
    VE.run(lambda e: e.memset(qthi[64:128, :], 0.0), inc=sV)
    VE.run(lambda e: e.memset(qtlo[64:128, :], 0.0), inc=sV)
    VE.run(lambda e: e.memset(rlst, 0.0), inc=sV)
    SY.wait(sKQ[0], sKQ[0].n)
    SY.wait(sKQ[1], sKQ[1].n)
    SY.wait(sVD[0], sVD[0].n)
    SY.wait(sVD[1], sVD[1].n)
    hs_ = {
        "ktile_rd": [wq_rd[0], wq_rd[1]], "qtile_rd": [wq_rd[0], wq_rd[1]], "vtile_rd": [0, 0],
        "in_k": [0, 0], "in_q": [0, 0], "in_v": [0, 0],
        "pt_rd": [0, 0], "exp_done": [0, 0], "score_done": [0, 0],
        "av_evac": [0, 0], "bc_free": 0, "ot_dma": [0, 0], "pending_evac": None,
    }
    TE.wait(sV, sV.n)
    TE.wait(sS, sS.n)

    def plan_prep(h):
        hg, hl, hb = h // 4, h % 4, h % 2
        SY.wait(sCC, 2 * hg + 2)
        SY.wait(sT, hs_["ktile_rd"][hb])
        for sd in range(2):
            for dt, kdst in ((0, kthi), (1, ktlo)):
                SY.run(lambda e, hg=hg, hl=hl, hb=hb, sd=sd, dt=dt, kdst=kdst: e.dma_start(
                    out=kdst[0:64, hb * S + sd * T: hb * S + (sd + 1) * T],
                    in_=kvk[hg].ap()[sd * 256 + hl * 64: sd * 256 + (hl + 1) * 64,
                                     dt * T:(dt + 1) * T]), inc=sK[hb], amt=16)
        hs_["in_k"][hb] = sK[hb].n
        SY.wait(sT, hs_["qtile_rd"][hb])
        for dt, qdst in ((0, qthi), (1, qtlo)):
            SY.run(lambda e, hg=hg, hl=hl, hb=hb, dt=dt, qdst=qdst: e.dma_start(
                out=qdst[0:64, hb * T:(hb + 1) * T],
                in_=qsp[hg].ap()[hl * 64:(hl + 1) * 64, dt * T:(dt + 1) * T]), inc=sQL[hb], amt=16)
        hs_["in_q"][hb] = sQL[hb].n
        SY.wait(sT, hs_["vtile_rd"][hb])
        for sd in range(2):
            SY.run(lambda e, hg=hg, hl=hl, hb=hb, sd=sd: e.dma_start(
                out=vtile[:, hb * 2080:(hb + 1) * 2080].rearrange("p (s f) -> p s f", f=65)[:, sd * 16:(sd + 1) * 16, 0:64],
                in_=kvv[hg].ap()[(sd * 4 + hl) * T: (sd * 4 + hl + 1) * T, :].rearrange("(j p) f -> p j f", p=128)),
                inc=sVL[hb], amt=16)
        hs_["in_v"][hb] = sVL[hb].n

    def plan_head(h):
        hg, hl, hb = h // 4, h % 4, h % 2
        kth = kthi[:, hb * S:(hb + 1) * S]
        ktl = ktlo[:, hb * S:(hb + 1) * S]
        qth = qthi[:, hb * T:(hb + 1) * T]
        qtl = qtlo[:, hb * T:(hb + 1) * T]
        vt = vtile[:, hb * 2080:(hb + 1) * 2080]
        TE.wait(sK[hb], hs_["in_k"][hb])
        TE.wait(sQL[hb], hs_["in_q"][hb])
        TE.wait(sVL[hb], hs_["in_v"][hb])
        for i4 in range(4):
            jmax = 4 * i4 + 3
            chunks = [(sd, j) for sd in range(2) for j in range(jmax + 1)]
            groups = [chunks[i:i + 2] for i in range(0, len(chunks), 2)]
            qsh = qth[:, i4 * TT:(i4 + 1) * TT]
            qsl = qtl[:, i4 * TT:(i4 + 1) * TT]
            ab = (h * 4 + i4) % 2
            av_first = [True]

            def plan_scores(gi, grp):
                bs = gi % 2
                TE.wait(sS, hs_["exp_done"][bs])
                for ci, (sd, j) in enumerate(grp):
                    pb = ps[:, bs * 1024 + ci * 512: bs * 1024 + (ci + 1) * 512]
                    kch = kth[:, (sd * 16 + j) * 128: (sd * 16 + j + 1) * 128]
                    kcl = ktl[:, (sd * 16 + j) * 128: (sd * 16 + j + 1) * 128]
                    TE.run(lambda e, pb=pb, kch=kch, qsh=qsh: e.matmul(pb, kch, qsh, start=True, stop=False))
                    TE.run(lambda e, pb=pb, kch=kch, qsl=qsl: e.matmul(pb, kch, qsl, start=False, stop=False))
                    hs_["score_done"][bs] = TE.run(
                        lambda e, pb=pb, kcl=kcl, qsh=qsh: e.matmul(pb, kcl, qsh, start=False, stop=True),
                        inc=sT)

            def plan_exp(gi, grp):
                bs = gi % 2
                SC.wait(sT, hs_["score_done"][bs])
                SC.wait(sT, hs_["pt_rd"][bs])
                n = len(grp)
                hs_["exp_done"][bs] = SC.run(lambda e, bs=bs, n=n: e.activation(
                    ptile[:, bs * 1024: bs * 1024 + n * 512],
                    ps[:, bs * 1024: bs * 1024 + n * 512], AF.Exp, scale=0.125), inc=sS)
                vmark = 0
                for ci, (sd, j) in enumerate(grp):
                    qb = j - 4 * i4
                    if 0 <= qb <= 3:
                        if vmark == 0:
                            VE.wait(sS, hs_["exp_done"][bs])
                        mk = maskt[:, sd * 128:(sd + 1) * 128]
                        pslice = ptile[:, bs * 1024 + ci * 512 + qb * 128: bs * 1024 + ci * 512 + (qb + 1) * 128]
                        vmark = VE.run(lambda e, pslice=pslice, mk=mk: e.tensor_tensor(
                            pslice, pslice, mk, ALU.mult), inc=sV)
                return vmark

            def plan_av(gi, grp, vmark):
                bs = gi % 2
                TE.wait(sS, hs_["exp_done"][bs])
                if vmark:
                    TE.wait(sV, vmark)
                if av_first[0]:
                    TE.wait(sV, hs_["av_evac"][ab])
                mark = 0
                for ci, (sd, j) in enumerate(grp):
                    startq = 128 * max(0, j - 4 * i4)
                    pl = ptile[:, bs * 1024 + ci * 512 + startq: bs * 1024 + (ci + 1) * 512]
                    vs_ = vt[:, (sd * 16 + j) * 65: (sd * 16 + j) * 65 + 65]
                    st = av_first[0]
                    av_first[0] = False
                    mark = TE.run(lambda e, pl=pl, vs_=vs_, st=st, startq=startq, ab=ab: e.matmul(
                        ps[0:65, (4 + ab) * 512 + startq: (5 + ab) * 512], vs_, pl,
                        start=st, stop=False, skip_group_check=True), inc=sT)
                hs_["pt_rd"][bs] = mark

            vmarks = {}
            pending = None
            for gi, grp in enumerate(groups):
                plan_scores(gi, grp)
                vmarks[gi] = plan_exp(gi, grp)
                if gi == 1 and hs_["pending_evac"] is not None:
                    hs_["pending_evac"]()
                    hs_["pending_evac"] = None
                if pending is not None:
                    plan_av(pending, groups[pending], vmarks[pending])
                pending = gi
            plan_av(pending, groups[pending], vmarks[pending])
            if hs_["pending_evac"] is not None:
                hs_["pending_evac"]()
                hs_["pending_evac"] = None
            t_av = sT.n

            def evac(h=h, i4=i4, ab=ab, t_av=t_av):
                ob = ab
                VE.wait(sT, t_av)
                VE.wait(sOT[ob], hs_["ot_dma"][ob])
                v_rl = VE.run(lambda e, ob=ob, ab=ab: e.reciprocal(rlst[0:1, ob * 512:(ob + 1) * 512],
                                                                   ps[64:65, (4 + ab) * 512:(5 + ab) * 512]), inc=sV)
                TE.wait(sV, v_rl)
                TE.wait(sV, hs_["bc_free"])
                t_bc2 = TE.run(lambda e, ob=ob: e.matmul(ps[:, 3072:3584], bc1,
                                                         rlst[:, ob * 512:(ob + 1) * 512],
                                                         start=True, stop=True), inc=sT)
                VE.run(lambda e, ob=ob, ab=ab: e.tensor_copy(otst[0:64, ob * 512:(ob + 1) * 512],
                                                             ps[0:64, (4 + ab) * 512:(5 + ab) * 512]), inc=sV)
                hs_["av_evac"][ab] = sV.n
                VE.wait(sT, t_bc2)
                v_div = VE.run(lambda e, ob=ob: e.tensor_tensor(
                    otst[0:64, ob * 512:(ob + 1) * 512], otst[0:64, ob * 512:(ob + 1) * 512],
                    ps[0:64, 3072:3584], ALU.mult), inc=sV)
                hs_["bc_free"] = v_div
                GP.wait(sV, v_div)
                GP.run(lambda e, h=h, i4=i4, ob=ob: e.dma_start(
                    out=otd.ap()[h * 64:(h + 1) * 64, i4 * TT:(i4 + 1) * TT],
                    in_=otst[0:64, ob * 512:(ob + 1) * 512]), inc=sOT[ob], amt=16)
                hs_["ot_dma"][ob] = sOT[ob].n

            hs_["pending_evac"] = evac
        hs_["ktile_rd"][hb] = sT.n
        hs_["qtile_rd"][hb] = sT.n
        hs_["vtile_rd"][hb] = sT.n

    plan_prep(0)
    SY.wait(sX1, 16)
    for h in range(NHEAD):
        if h + 1 < NHEAD:
            plan_prep(h + 1)
        if h < C:
            SY.run(lambda e, h=h: e.dma_start(
                out=x1f[:, h * T:(h + 1) * T],
                in_=x1sp.rearrange("(c p) t -> p c t", p=128)[:, h, :]), inc=sX1L, amt=16)
            ld["x1f"] = sX1L.n
        plan_head(h)

    if hs_["pending_evac"] is not None:
        hs_["pending_evac"]()
        hs_["pending_evac"] = None

    v_attn_done = sV.n
    TE.wait(sS, sS.n)
    TE.wait(sV, v_attn_done)
    SY.wait(sV, v_attn_done)
    SY.wait(sX1, 16)
    SY.wait(sT, sT.n)
    SY.wait(sOT[0], sOT[0].n)
    SY.wait(sOT[1], sOT[1].n)
    sydma("wo", wout_s.rearrange("p (c t) -> p c t", t=D), wout_d.rearrange("c p t -> p c t"), sem=sLD)
    otf_c = otf.rearrange("p (c t) -> p c t", t=T)
    otd_c = otd.ap().rearrange("(c p) t -> p c t", p=128)
    SY.run(lambda e: e.dma_start(out=otf_c[:, 0:4, :], in_=otd_c[:, 0:4, :]), inc=sOTL, amt=16)
    ld["otf"] = sOTL.n
    GP.wait(sOT[0], sOT[0].n)
    GP.wait(sOT[1], sOT[1].n)
    GP.run(lambda e: e.dma_start(out=otf_c[:, 4:8, :], in_=otd_c[:, 4:8, :]), inc=sX2L, amt=16)
    ld["otf2"] = sX2L.n
    TE.wait(sLD, ld["wo"])
    TE.wait(sOTL, ld["otf"])
    for mc in range(C):
        bs = (mc % 2) * 4
        banks = [bs, bs + 1, bs + 2, bs + 3]
        for tt in range(NTT):
            wev(TE, banks[tt])
        for c in range(C):
            if mc == 0 and c == 4:
                TE.wait(sX2L, ld["otf2"])
            wc = wout_s[:, c * D + mc * 128: c * D + (mc + 1) * 128]
            for tt in range(NTT):
                st = (c == 0)
                oc = otf[:, c * T + tt * TT: c * T + tt * TT + TT]
                TE.run(lambda e, wc=wc, oc=oc, st=st, b=banks[tt], lastc=(c == C - 1): e.matmul(
                    ps[:, b * 512:(b + 1) * 512], wc, oc, start=st, stop=lastc),
                    inc=(sT if (c == C - 1 and tt == NTT - 1) else None))
        t_mm = sT.n
        VE.wait(sT, t_mm)
        VE.wait(sX1L, ld["x1f"])
        ob_col = obias[:, mc:mc + 1]
        for tt in range(NTT):
            dst = x1f[:, mc * T + tt * TT: mc * T + tt * TT + TT]
            VE.run(lambda e, b=banks[tt], dst=dst, ob_col=ob_col: e.scalar_tensor_tensor(
                dst, ps[:, b * 512:(b + 1) * 512], ob_col, dst, ALU.add, ALU.add), inc=sV)
            st8["evac"][banks[tt]] = (sV, sV.n)
    v_x2 = sV.n
    if dump == "x2":
        GP.wait(sV, v_x2)
        GP.run(lambda e: e.dma_start(out=x2sp.rearrange("(c p) t -> p c t", p=128),
                                     in_=x1f.rearrange("p (c t) -> p c t", t=T)), inc=sX2, amt=16)
        GP.wait(sX2, 16)
        flush(nc, SY, GP, VE, SC, TE, audit)
        return nc

    t_p4 = sT.n
    v_p4 = sV.n
    VE.wait(sT, t_p4)
    VE.wait(sOT[0], sOT[0].n)
    VE.wait(sOT[1], sOT[1].n)
    SC.wait(sT, t_p4)
    SC.wait(sOT[0], sOT[0].n)
    SC.wait(sOT[1], sOT[1].n)
    SY.wait(sT, t_p4)
    SY.wait(sV, v_p4)
    SY.run(lambda e: e.dma_start(out=qw16, in_=qd.ap()), inc=sQWL, amt=16)
    qw_ld = sQWL.n
    v_ln3, ta_ln3, _ = plan_ln_phase(x1f, None, pre_wait=[(sV, v_p4)])
    TE.wait(sQWL, qw_ld)

    def ffn2_evac(mc, tt, pb, t_mm, bank):
        VE.wait(sT, t_mm)
        dst = x1f[:, mc * T + tt * TT: mc * T + tt * TT + TT]
        VE.run(lambda e, pb=pb, dst=dst: e.scalar_tensor_tensor(dst, pb, HS, dst, ALU.mult, ALU.add), inc=sV)
        st8["evac"][bank] = (sV, sV.n)

    plan_ffn_all(ffn2_evac, v_ln3, ta_ln3)
    GP.wait(sV, sV.n)
    GP.run(lambda e: e.dma_start(out=y_d.rearrange("(c p) t -> p c t", p=128),
                                 in_=x1f.rearrange("p (c t) -> p c t", t=T)), inc=sOUT, amt=16)
    GP.wait(sOUT, 16)
    flush(nc, SY, GP, VE, SC, TE, audit)
    return nc


def flush(nc, SY, GP, VE, SC, TE, audit=False):
    def mk(q, drain_inc=False):
        def body(eng):
            emitted = {}
            for it in q.items:
                if it[0] == "w":
                    eng.wait_ge(it[1], it[2])
                else:
                    ins = it[1](eng)
                    if it[2] is not None:
                        h, cum = it[2]
                        if drain_inc:
                            eng.drain().then_inc(h, cum - emitted.get(id(h), 0))
                            emitted[id(h)] = cum
                        else:
                            ins.then_inc(h, cum - emitted.get(id(h), 0))
                            emitted[id(h)] = cum
                    elif drain_inc:
                        eng.drain()
        return body

    with nc.Block() as block:
        block.sync(mk(SY))
        block.gpsimd(mk(GP))
        block.vector(mk(VE, drain_inc=True))
        block.scalar(mk(SC, drain_inc=True))
        block.tensor(mk(TE))


def prep_inputs(inputs):
    x = np.ascontiguousarray(np.asarray(inputs["L0_attn_x"], dtype=np.float32))
    qkv_w = np.asarray(inputs["L0_attn_qkv_w"], dtype=np.float32)
    qkv_b = np.asarray(inputs["L0_attn_qkv_b"], dtype=np.float32)
    out_w = np.asarray(inputs["L0_attn_out_w"], dtype=np.float32)
    out_b = np.asarray(inputs["L0_attn_out_b"], dtype=np.float32)
    fk = np.asarray(inputs["L0_ffn_kernel"], dtype=np.float32)
    fbias = np.asarray(inputs["L0_ffn_bias"], dtype=np.float32)
    fa = np.asarray(inputs["L0_ffn_a"], dtype=np.float32)
    fb = np.asarray(inputs["L0_ffn_b"], dtype=np.float32)

    qwm, kwm, vwm = qkv_w[:, 0:D], qkv_w[:, D:2 * D], qkv_w[:, 2 * D:3 * D]
    qb_, kb_, vb_ = qkv_b[0:D], qkv_b[D:2 * D], qkv_b[2 * D:3 * D]
    wqkv = np.empty((HG, C, 128, 768), np.float32)
    qkvbc = np.empty((HG, 6, 128), np.float32)
    for g in range(HG):
        cols = np.concatenate([kwm[:, g * 256:(g + 1) * 256],
                               vwm[:, g * 256:(g + 1) * 256],
                               qwm[:, g * 256:(g + 1) * 256]], axis=1)
        wqkv[g] = cols.reshape(C, 128, 768)
        qkvbc[g] = np.concatenate([kb_[g * 256:(g + 1) * 256],
                                   vb_[g * 256:(g + 1) * 256],
                                   qb_[g * 256:(g + 1) * 256]]).reshape(6, 128)
    common = {
        "wqkv": wqkv,
        "qkvbc": np.ascontiguousarray(qkvbc.reshape(HG * 6, 128).T),
        "wout": out_w.reshape(C, 128, D),
        "outbc": np.ascontiguousarray(out_b.reshape(C, 128).T),
        "fker": fk.reshape(C, 128, D),
        "fbias": fbias.reshape(1, D),
        "fa": fa.reshape(C, 128, 16),
        "fb": fb,
        "cones": np.ones((1, T), np.float32),
        "idf": np.eye(128, dtype=np.float32),
    }
    tri = np.triu(np.ones((128, 128), np.float32))
    in_maps = []
    for core in range(8):
        b, p = core // 2, core % 2
        xb = x[b].reshape(32, 128, D)[p::2].reshape(T, D)
        m = np.empty((128, 256), np.float32)
        if p == 0:
            m[:, 0:128] = tri
            m[:, 128:256] = 0.0
        else:
            m[:, 0:128] = 1.0
            m[:, 128:256] = tri
        im = dict(common)
        im["x0"] = np.ascontiguousarray(xb.T)
        im["maskio"] = m
        in_maps.append(im)
    return in_maps


def assemble(results):
    out = np.empty((B, S, D), np.float32)
    for core in range(8):
        b, p = core // 2, core % 2
        yc = results[core]["y"]
        out[b].reshape(32, 128, D)[p::2] = yc.T.reshape(16, 128, D)
    return out


def kernel(**inputs):
    in_maps = prep_inputs(inputs)
    nc = build_nc()
    res = run_bass_kernel_spmd(nc, in_maps, core_ids=list(range(8)))
    return assemble(res.results)


# version 29
# speedup vs baseline: 1.0317x; 1.0317x over previous
import numpy as np
import ml_dtypes
import concourse.bass as bass
import concourse.mybir as mybir
from concourse.bass_utils import run_bass_kernel_spmd

F32 = mybir.dt.float32
F16 = mybir.dt.float16
F8 = mybir.dt.float8e4
AF = mybir.ActivationFunctionType
ALU = mybir.AluOpType
AX = mybir.AxisListType.X

B, S, D = 4, 4096, 1024
NHEAD, HD = 16, 64
T, C, TT, NTT = 2048, 8, 512, 4
HG, HPG = 4, 4
EPS = 1e-5
RG = [[0, 1], [2, 3], [4, 5], [6, 7]]


class Sem:
    def __init__(self, nc, name):
        self.h = nc.alloc_semaphore(name)
        self.n = 0


class EQ:
    def __init__(self):
        self.items = []

    def wait(self, sem, val):
        if val > 0:
            self.items.append(("w", sem.h, val))

    def run(self, fn, inc=None, amt=1):
        tgt = None
        if inc is not None:
            inc.n += amt
            tgt = (inc.h, inc.n)
        self.items.append(("r", fn, tgt))
        return inc.n if inc is not None else None


def build_nc(dump=None, audit=False):
    nc = bass.Bass("TRN2", target_bir_lowering=False)

    def din(name, shape, dtype=F32):
        return nc.dram_tensor(name, shape, dtype, kind="ExternalInput").ap()

    x0_d = din("x0", [D, T])
    wqkv_d = din("wqkv", [HG, C, 128, 768])
    qkvbc_d = din("qkvbc", [128, HG * 6])
    wout_d = din("wout", [C, 128, D])
    outbc_d = din("outbc", [128, C])
    fker_d = din("fker", [C, 128, D])
    fbias_d = din("fbias", [1, D])
    fa_d = din("fa", [C, 128, 16])
    fb_d = din("fb", [16, D])
    cones_d = din("cones", [1, T])
    mask_d = din("maskio", [128, 256])
    idf_d = din("idf", [128, 128])

    y_d = nc.dram_tensor("y", [D, T], F32, kind="ExternalOutput").ap()

    agink = [nc.dram_tensor(f"agink{g}", [256, 2 * T], F16, kind="Internal") for g in range(HG)]
    aginv = [nc.dram_tensor(f"aginv{g}", [HPG * T, HD], F32, kind="Internal") for g in range(HG)]
    kvk = [nc.dram_tensor(f"kvk{g}", [512, 2 * T], F16, kind="Internal") for g in range(HG)]
    kvv = [nc.dram_tensor(f"kvv{g}", [2 * HPG * T, HD], F32, kind="Internal") for g in range(HG)]
    qsp = [nc.dram_tensor(f"qsp{g}", [256, 2 * T], F16, kind="Internal") for g in range(HG)]
    x1sp = y_d if dump == "x1" else nc.dram_tensor("x1sp", [D, T], F32, kind="Internal").ap()
    x2sp = y_d if dump == "x2" else nc.dram_tensor("x2sp", [D, T], F32, kind="Internal").ap()
    scr128 = nc.dram_tensor("scr128", [128], F32, kind="Internal")
    otd = nc.dram_tensor("otd", [D, T], F32, kind="Internal")
    qd = nc.dram_tensor("qd", [128, C * D], F16, kind="Internal")

    A = nc.alloc_sbuf_tensor
    af = A("af", [128, 128], F32).ap()
    ahi = A("ahi", [128, 128], F16).ap()
    alo = A("alo", [128, 128], F16).ap()
    beff = A("beff", [128, D], F32).ap()
    uo = A("uo", [128, T], F32).ap()
    qbias = A("qbias", [128, HG * 6], F32).ap()
    obias = A("obias", [128, C], F32).ap()
    maskt = A("maskt", [128, 256], F32).ap()
    idf = A("idfs", [128, 128], F32).ap()
    onesf = A("onesfp", [128, 512], F32).ap()
    bc1 = A("bc1", [128, 128], F32).ap()
    bcst = A("bcst", [128, 8], F32).ap()
    smalls = A("smalls", [1, 1152], F32).ap()
    gmax8 = A("gmax8", [128, 16], F32).ap()
    rowv = A("rowv", [1, 128], F32).ap()

    AB = (nc.sbuf_base + 31) & ~31
    ARENA = nc.sbuf_top - AB

    def at(name, shape, dtype, off):
        esz = {F16: 2, F8: 1}.get(dtype, 4)
        end = off + shape[1] * esz
        assert end <= ARENA, f"{name}: arena overflow {end} > {ARENA}"
        return nc.alloc_sbuf_tensor_at(name, shape, dtype, offset=AB + off).ap()

    xs = at("xs", [128, C * T], F32, 0)
    x1f = at("x1f", [128, C * T], F32, 0)
    qw = at("qw", [128, C * D], F32, 65536)
    qw16 = at("qw16", [128, C * D], F16, 65536)
    ln_sq = at("ln_sq", [128, 4096], F32, 98304)
    ln_st = at("ln_st", [128, 4 * 512], F32, 114688)
    xhi = at("xhi", [128, C * T], F16, 122880)
    xlo = at("xlo", [128, C * T], F16, 155648)
    qsc = at("qsc", [128, 4096], F32, 98304)
    q8a = at("q8a", [128, 1024], F8, 110592)
    q8b = at("q8b", [128, 1024], F8, 111616)
    wstag = at("wstag", [128, 2 * C * 768], F32, 65536)
    spst = at("spst", [128, 2 * T], F32, 114688)
    vtr2 = at("vtr2", [128, 2 * T], F32, 131072)
    sp16 = at("sp16", [128, 2 * 2 * T], F16, 147456)
    kthi = at("kthi", [128, 2 * S], F16, 65536)
    ktlo = at("ktlo", [128, 2 * S], F16, 81920)
    qthi = at("qthi", [128, 2 * T], F16, 98304)
    qtlo = at("qtlo", [128, 2 * T], F16, 106496)
    vtile = at("vtile", [128, 2 * 32 * 65], F32, 114688)
    ptile = at("ptile", [128, 2 * 2 * 512], F32, 131392)
    otst = at("otst", [128, 2 * 512], F32, 139712)
    rlst = at("rlst", [128, 2 * 512], F32, 143808)
    otf = at("otf", [128, C * T], F32, 65536)
    wout_s = at("wout_s", [128, C * D], F32, 131072)

    ps = nc.alloc_psum_tensor("ps", [128, 4096], F32).ap()

    sLD, sCC = Sem(nc, "sLD"), Sem(nc, "sCC")
    sGB, sX1, sX2, sQW = Sem(nc, "sGB"), Sem(nc, "sX1"), Sem(nc, "sX2"), Sem(nc, "sQW")
    sKQ = [Sem(nc, "sKQ0"), Sem(nc, "sKQ1")]
    sVD = [Sem(nc, "sVD0"), Sem(nc, "sVD1")]
    sWQ = [Sem(nc, "sWQ0"), Sem(nc, "sWQ1")]
    sK = [Sem(nc, "sK0"), Sem(nc, "sK1")]
    sQL = [Sem(nc, "sQL0"), Sem(nc, "sQL1")]
    sVL = [Sem(nc, "sVL0"), Sem(nc, "sVL1")]
    sX1L, sOTL, sQWL = Sem(nc, "sX1L"), Sem(nc, "sOTL"), Sem(nc, "sQWL")
    sGL = Sem(nc, "sGL")
    sXT = [Sem(nc, f"sXT{i}") for i in range(NTT)]
    sOT = [Sem(nc, "sOT0"), Sem(nc, "sOT1")]
    sX2L = Sem(nc, "sX2L")
    sT, sV, sS = Sem(nc, "sT"), Sem(nc, "sV"), Sem(nc, "sS")
    sOUT = Sem(nc, "sOUT")
    SY, GP, VE, SC, TE = EQ(), EQ(), EQ(), EQ(), EQ()

    ld = {}

    def sydma(key, out, in_, sem=sLD):
        SY.run(lambda e, out=out, in_=in_: e.dma_start(out=out, in_=in_), inc=sem, amt=16)
        ld[key] = sem.n

    xs_c = xs.rearrange("p (c t) -> p c t", t=T)
    x0_c = x0_d.rearrange("(c p) t -> p c t", p=128)
    ld_t = []
    for tt in range(NTT):
        q_ = SY if tt < 2 else GP
        sm = sXT[tt]
        q_.run(lambda e, tt=tt: e.dma_start(out=xs_c[:, :, tt * TT:(tt + 1) * TT],
                                            in_=x0_c[:, :, tt * TT:(tt + 1) * TT]), inc=sm, amt=16)
        ld_t.append((sm, sm.n))
    GP.run(lambda e: e.dma_start(out=qw.rearrange("p (c t) -> p c t", t=D),
                                 in_=fker_d.rearrange("c p t -> p c t")), inc=sGL, amt=16)
    ld_fk = sGL.n
    sydma("mask", maskt, mask_d)
    sydma("idf", idf, idf_d)
    sydma("fa", af.rearrange("p (c t) -> p c t", t=16), fa_d.rearrange("c p t -> p c t"))
    SY.wait(sV, 5)
    sydma("fb", beff[0:16, :], fb_d)
    sydma("fbias", beff[16:17, :], fbias_d)
    sydma("cones", uo[16:17, :], cones_d)
    sydma("qbias", qbias, qkvbc_d)
    sydma("obias", obias, outbc_d)
    ld_b1 = sLD.n

    VE.run(lambda e: e.memset(onesf, 1.0), inc=sV)
    VE.run(lambda e: e.memset(beff, 0.0), inc=sV)
    VE.run(lambda e: e.memset(uo, 0.0), inc=sV)
    VE.run(lambda e: e.memset(bc1, 0.0), inc=sV)
    v_z0 = VE.run(lambda e: e.memset(bc1[0:1, 0:64], 1.0), inc=sV)

    st8 = {
        "ln_sq_free": (sV, 0), "v_af": 0,
        "evac": {b: None for b in range(8)},
        "u_rd": 0,
    }

    def wev(q, bank):
        m = st8["evac"][bank]
        if m is not None:
            q.wait(m[0], m[1])

    def plan_quant():
        q_t, q_m = qsc[:, 0:1024], qsc[:, 1024:2048]
        v_qc = []
        for c in range(C):
            wc = qw[:, c * D:(c + 1) * D]
            VE.run(lambda e, wc=wc: e.tensor_scalar(q_t, wc, RS, None, ALU.mult), inc=sV)
            VE.run(lambda e: e.tensor_scalar(q8a, q_t, 0.0, 8.0, ALU.max, ALU.add), inc=sV)
            VE.run(lambda e: e.tensor_scalar(q_m, q_t, 0.0, -1.0, ALU.min, ALU.mult), inc=sV)
            VE.run(lambda e: e.tensor_single_scalar(q8b, q_m, 8.0, ALU.add), inc=sV)
            v_qc.append((sV, VE.run(lambda e, c=c: e.tensor_tensor(
                qw16[:, c * D:(c + 1) * D], q8a, q8b, ALU.subtract), inc=sV)))
        GP.wait(sV, v_qc[-1][1])
        GP.run(lambda e: e.dma_start(out=qd.ap(), in_=qw16), inc=sQW, amt=16)
        return v_qc

    def plan_ln_phase(src, dst, pre_wait=None, gp_wait=None, tile_pre=None):
        sq = ln_sq
        t_stats = [0] * NTT
        for tt in range(NTT):
            if tt == 0:
                for w in (pre_wait or []):
                    SC.wait(*w)
            if tile_pre is not None:
                SC.wait(*tile_pre[tt])
            SC.wait(*st8["ln_sq_free"])
            s_sq = 0
            for c in range(C):
                xc = src[:, c * T + tt * TT: c * T + tt * TT + TT]
                s_sq = SC.run(lambda e, xc=xc, c=c: e.activation(
                    sq[:, c * 512:(c + 1) * 512], xc, AF.Square), inc=sS)
            if tt == 0:
                for w in (pre_wait or []):
                    TE.wait(*w)
            if tile_pre is not None:
                TE.wait(*tile_pre[tt])
            wev(TE, 2 * tt)
            wev(TE, 2 * tt + 1)
            TE.wait(sS, s_sq)
            for c in range(C):
                xc = src[:, c * T + tt * TT: c * T + tt * TT + TT]
                st = (c == 0)
                last = (c == C - 1)
                TE.run(lambda e, xc=xc, st=st, last=last, tt=tt: e.matmul(
                    ps[:, tt * 1024: tt * 1024 + 512], onesf[:, 0:128], xc, start=st, stop=last))
                TE.run(lambda e, c=c, st=st, last=last, tt=tt: e.matmul(
                    ps[:, tt * 1024 + 512:(tt + 1) * 1024], onesf[:, 0:128], sq[:, c * 512:(c + 1) * 512],
                    start=st, stop=last), inc=sT)
            t_stats[tt] = sT.n
            st8["ln_sq_free"] = (sT, t_stats[tt])
        v_apply = (sV, 0)
        tile_apply = []
        for tt in range(NTT):
            m_, ve_, r_, b_ = (ln_st[:, 0:512], ln_st[:, 512:1024], ln_st[:, 1024:1536], ln_st[:, 1536:2048])
            VE.wait(sT, t_stats[tt])
            VE.run(lambda e, tt=tt: e.tensor_single_scalar(m_, ps[:, tt * 1024: tt * 1024 + 512],
                                                           1.0 / D, ALU.mult), inc=sV)
            VE.run(lambda e, tt=tt: e.tensor_single_scalar(ve_, ps[:, tt * 1024 + 512:(tt + 1) * 1024],
                                                           1.0 / D, ALU.mult), inc=sV)
            st8["evac"][2 * tt] = (sV, sV.n)
            st8["evac"][2 * tt + 1] = (sV, sV.n)
            VE.run(lambda e: e.tensor_tensor(r_, m_, m_, ALU.mult), inc=sV)
            VE.run(lambda e: e.tensor_tensor(ve_, ve_, r_, ALU.subtract), inc=sV)
            VE.run(lambda e: e.tensor_single_scalar(ve_, ve_, EPS, ALU.add), inc=sV)
            v_rc = VE.run(lambda e: e.reciprocal(ve_, ve_), inc=sV)
            SC.wait(sV, v_rc)
            s_r = SC.run(lambda e: e.activation(r_, ve_, AF.Sqrt), inc=sS)
            VE.wait(sS, s_r)
            VE.run(lambda e: e.scalar_tensor_tensor(b_, m_, -1.0, r_, ALU.mult, ALU.mult), inc=sV)
            if gp_wait is not None and tt == 0:
                VE.wait(*gp_wait)
            if dst is not None:
                for c in range(C):
                    xc = src[:, c * T + tt * TT: c * T + tt * TT + TT]
                    dc = dst[:, c * T + tt * TT: c * T + tt * TT + TT]
                    VE.run(lambda e, dc=dc, xc=xc: e.tensor_tensor(dc, xc, r_, ALU.mult), inc=sV)
                    va = VE.run(lambda e, dc=dc: e.tensor_tensor(dc, dc, b_, ALU.add), inc=sV)
                v_apply = (sV, va)
                tile_apply.append(v_apply)
                continue
            slots = (m_, ve_)
            s_cast = {}
            for c in range(C):
                xc = src[:, c * T + tt * TT: c * T + tt * TT + TT]
                hc = xhi[:, c * T + tt * TT: c * T + tt * TT + TT]
                lc = xlo[:, c * T + tt * TT: c * T + tt * TT + TT]
                sl = slots[c % 2]
                VE.run(lambda e, sl=sl, xc=xc: e.tensor_tensor(sl, xc, r_, ALU.mult), inc=sV)
                v_st = VE.run(lambda e, sl=sl: e.tensor_tensor(sl, sl, b_, ALU.add), inc=sV)
                SC.wait(sV, v_st)
                s_cast[c] = SC.run(lambda e, hc=hc, sl=sl: e.activation(hc, sl, AF.Copy), inc=sS)
                if c >= 1:
                    VE.wait(sS, s_cast[c - 1])
                    pc = c - 1
                    hp = xhi[:, pc * T + tt * TT: pc * T + tt * TT + TT]
                    lp = xlo[:, pc * T + tt * TT: pc * T + tt * TT + TT]
                    va = VE.run(lambda e, lp=lp, hp=hp, sl=slots[pc % 2]: e.tensor_tensor(
                        lp, sl, hp, ALU.subtract), inc=sV)
            VE.wait(sS, s_cast[C - 1])
            lce = xlo[:, (C - 1) * T + tt * TT: (C - 1) * T + tt * TT + TT]
            hce = xhi[:, (C - 1) * T + tt * TT: (C - 1) * T + tt * TT + TT]
            va = VE.run(lambda e, sl=slots[(C - 1) % 2], lce=lce, hce=hce: e.tensor_tensor(
                lce, sl, hce, ALU.subtract), inc=sV)
            v_apply = (sV, va)
            tile_apply.append(v_apply)
        return v_apply, tile_apply, t_stats[NTT - 1]

    def plan_ffn_all(evac_fn, v_xln, tiles, qwait=None):
        for b in (4, 5, 6, 7):
            wev(TE, b)
        TE.wait(sV, st8["v_af"])
        for tt in range(NTT):
            TE.wait(*tiles[tt])
            for c in range(C):
                for pi, (am, xm) in enumerate(((ahi, xhi), (ahi, xlo), (alo, xhi))):
                    ac = am[:, c * 16:(c + 1) * 16]
                    st = (c == 0 and pi == 0)
                    last = (c == C - 1 and pi == 2)
                    xc = xm[:, c * T + tt * TT: c * T + tt * TT + TT]
                    TE.run(lambda e, ac=ac, xc=xc, st=st, tt=tt, last=last: e.matmul(
                        ps[0:16, 2048 + tt * 512: 2048 + (tt + 1) * 512], ac, xc, start=st,
                        stop=last), inc=(sT if (last and tt == NTT - 1) else None))
        TE.wait(*v_xln)
        t_u = sT.n
        SC.wait(sT, t_u)
        SC.wait(sT, st8["u_rd"])
        v_u = SC.run(lambda e: e.activation(uo[0:16, :], ps[0:16, 2048:4096], AF.Copy), inc=sS)
        for b in (4, 5, 6, 7):
            st8["evac"][b] = (sS, v_u)
        for mc in range(C):
            bs = (mc % 2) * 4
            banks = [bs, bs + 1, bs + 2, bs + 3]
            for tt in range(NTT):
                wev(TE, banks[tt])
            if mc == 0:
                TE.wait(sS, v_u)
            bv = beff[:, mc * 128:(mc + 1) * 128]
            for tt in range(NTT):
                TE.run(lambda e, bv=bv, tt=tt, b=banks[tt]: e.matmul(
                    ps[:, b * 512:(b + 1) * 512], bv, uo[:, tt * 512:(tt + 1) * 512],
                    start=True, stop=False))
            marks = {}
            for c in range(C):
                if qwait is not None and mc == 0:
                    TE.wait(*qwait[c])
                qc = qw16[:, c * D + mc * 128: c * D + (mc + 1) * 128]
                for tt in range(NTT):
                    for pi, xm in enumerate((xhi, xlo)):
                        lastmm = (c == C - 1 and pi == 1)
                        xc = xm[:, c * T + tt * TT: c * T + tt * TT + TT]
                        TE.run(lambda e, qc=qc, xc=xc, b=banks[tt], lastmm=lastmm: e.matmul(
                            ps[:, b * 512:(b + 1) * 512], qc, xc, start=False, stop=lastmm),
                            inc=(sT if lastmm else None))
                        if lastmm:
                            marks[tt] = sT.n
            for tt in range(NTT):
                evac_fn(mc, tt, ps[:, banks[tt] * 512:(banks[tt] + 1) * 512], marks[tt], banks[tt])
        st8["u_rd"] = sT.n

    v_ln, ta_ln1, _ = plan_ln_phase(xs, None, pre_wait=[(sV, 5)], tile_pre=ld_t)
    VE.wait(sGL, ld_fk)
    VE.wait(sLD, ld_b1)
    for c in range(C):
        VE.run(lambda e, c=c: e.reduce_max(gmax8[:, c:c + 1], qw[:, c * D:(c + 1) * D], AX,
                                           apply_absolute_value=True), inc=sV)
    v_gmax = VE.run(lambda e: e.reduce_max(gmax8[:, 8:9], gmax8[:, 0:8], AX), inc=sV)
    GP.wait(sV, v_gmax)
    GP.run(lambda e: e.dma_start(out=scr128.ap().rearrange("(a b) -> a b", a=128), in_=gmax8[:, 8:9]),
           inc=sGB, amt=16)
    GP.wait(sGB, sGB.n)
    GP.run(lambda e: e.dma_start(out=rowv, in_=scr128.ap().rearrange("(a b) -> a b", a=1)), inc=sGB, amt=16)
    VE.wait(sGB, sGB.n)
    VE.run(lambda e: e.reduce_max(smalls[0:1, 1040:1041], rowv, AX), inc=sV)
    VE.run(lambda e: e.reciprocal(smalls[0:1, 1041:1042], smalls[0:1, 1040:1041]), inc=sV)
    VE.run(lambda e: e.tensor_single_scalar(smalls[0:1, 1042:1043], smalls[0:1, 1041:1042], 7.0, ALU.mult), inc=sV)
    v_rs = VE.run(lambda e: e.tensor_single_scalar(smalls[0:1, 1043:1044], smalls[0:1, 1040:1041],
                                                   1.0 / 14.0, ALU.mult), inc=sV)
    TE.wait(sV, v_rs)
    t_bc = TE.run(lambda e: e.matmul(ps[:, 0:2], onesf[0:1, 0:128], smalls[0:1, 1042:1044],
                                     start=True, stop=True), inc=sT)
    VE.wait(sT, t_bc)
    v_bc = VE.run(lambda e: e.tensor_copy(bcst[:, 0:2], ps[:, 0:2]), inc=sV)
    RS = bcst[:, 0:1]
    HS = bcst[:, 1:2]
    st8["evac"][0] = (sV, v_bc)

    VE.run(lambda e: e.tensor_scalar(beff[0:17, :], beff[0:17, :], bcst[0:17, 0:1], None, ALU.mult), inc=sV)
    VE.run(lambda e: e.tensor_copy(ahi, af), inc=sV)
    VE.run(lambda e: e.tensor_copy(ln_st[:, 0:128], ahi), inc=sV)
    v_af = VE.run(lambda e: e.tensor_tensor(alo, af, ln_st[:, 0:128], ALU.subtract), inc=sV)
    st8["v_af"] = v_af

    v_qc = plan_quant()

    g_x1 = {}

    def ffn1_evac(mc, tt, pb, t_mm, bank):
        VE.wait(sT, t_mm)
        dst = xs[:, mc * T + tt * TT: mc * T + tt * TT + TT]
        VE.run(lambda e, pb=pb, dst=dst: e.scalar_tensor_tensor(dst, pb, HS, dst, ALU.mult, ALU.add), inc=sV)
        st8["evac"][bank] = (sV, sV.n)

    plan_ffn_all(ffn1_evac, v_ln, ta_ln1, qwait=v_qc)
    v_x1 = sV.n
    GP.wait(sV, v_x1)
    GP.run(lambda e: e.dma_start(out=x1sp.rearrange("(c p) t -> p c t", p=128),
                                 in_=xs.rearrange("p (c t) -> p c t", t=T)), inc=sX1, amt=16)

    if dump == "x1":
        GP.wait(sX1, 16)
        flush(nc, SY, GP, VE, SC, TE, audit)
        return nc

    v_xln2, ta_ln2, t_ln2sums = plan_ln_phase(xs, xs, pre_wait=[(sV, sV.n)], gp_wait=(sX1, 16))

    wq_rd = {0: 0, 1: 0}
    sp_free = [(None, 0), (None, 0)]
    sp16_free = [(None, 0), (None, 0)]
    vt_free = [(None, 0), (None, 0)]
    for hg in range(HG):
        wb_ = hg % 2
        if hg == 0:
            SY.wait(sV, v_qc[-1][1])
            SY.wait(sT, sT.n)
            SY.wait(sS, sS.n)
            SY.wait(sQW, 16)
        SY.wait(sT, wq_rd[wb_])
        sydma(f"wq{hg}", wstag[:, wb_ * 6144:(wb_ + 1) * 6144].rearrange("p (c t) -> p c t", t=768),
              wqkv_d[hg].rearrange("c p t -> p c t"), sem=sWQ[wb_])
        TE.wait(sWQ[wb_], ld[f"wq{hg}"])
        for mc in range(6):
            sb = mc % 2
            bs = sb * 4
            banks = [bs, bs + 1, bs + 2, bs + 3]
            if hg == 0 and mc == 0:
                for tt in range(NTT):
                    wev(TE, banks[tt])
                    TE.wait(*ta_ln2[tt])
                    for c in range(C):
                        wc = wstag[:, wb_ * 6144 + c * 768 + mc * 128: wb_ * 6144 + c * 768 + (mc + 1) * 128]
                        xc = xs[:, c * T + tt * TT: c * T + tt * TT + TT]
                        TE.run(lambda e, wc=wc, xc=xc, st=(c == 0), b=banks[tt], lastc=(c == C - 1): e.matmul(
                            ps[:, b * 512:(b + 1) * 512], wc, xc, start=st, stop=lastc),
                            inc=(sT if (c == C - 1 and tt == NTT - 1) else None))
            else:
                for tt in range(NTT):
                    wev(TE, banks[tt])
                for c in range(C):
                    wc = wstag[:, wb_ * 6144 + c * 768 + mc * 128: wb_ * 6144 + c * 768 + (mc + 1) * 128]
                    for tt in range(NTT):
                        st = (c == 0)
                        xc = xs[:, c * T + tt * TT: c * T + tt * TT + TT]
                        TE.run(lambda e, wc=wc, xc=xc, st=st, b=banks[tt], lastc=(c == C - 1): e.matmul(
                            ps[:, b * 512:(b + 1) * 512], wc, xc, start=st, stop=lastc),
                            inc=(sT if (c == C - 1 and tt == NTT - 1) else None))
            t_mm = sT.n
            SC.wait(sT, t_mm)
            if sp_free[sb][0] is not None:
                SC.wait(*sp_free[sb])
            qb_col = qbias[:, hg * 6 + mc: hg * 6 + mc + 1]
            s_sp = 0
            for tt in range(NTT):
                s_sp = SC.run(lambda e, sb=sb, tt=tt, b=banks[tt], qb_col=qb_col: e.activation(
                    spst[:, sb * T + tt * 512: sb * T + (tt + 1) * 512],
                    ps[:, b * 512:(b + 1) * 512], AF.Identity, bias=qb_col), inc=sS)
                st8["evac"][banks[tt]] = (sS, s_sp)
            if mc < 2 or mc >= 4:
                VE.wait(sS, s_sp)
                if sp16_free[sb][0] is not None:
                    VE.wait(*sp16_free[sb])
                v_hl = 0
                for tt in range(NTT):
                    sv = spst[:, sb * T + tt * 512: sb * T + (tt + 1) * 512]
                    hv = sp16[:, sb * 2 * T + tt * 512: sb * 2 * T + (tt + 1) * 512]
                    lv = sp16[:, sb * 2 * T + T + tt * 512: sb * 2 * T + T + (tt + 1) * 512]
                    VE.run(lambda e, hv=hv, sv=sv: e.tensor_copy(hv, sv), inc=sV)
                    v_hl = VE.run(lambda e, lv=lv, sv=sv, hv=hv: e.tensor_tensor(
                        lv, sv, hv, ALU.subtract), inc=sV)
                sp_free[sb] = (sV, v_hl)
                GP.wait(sV, v_hl)
                dst = (agink[hg].ap()[mc * 128:(mc + 1) * 128, :] if mc < 2
                       else qsp[hg].ap()[(mc - 4) * 128:(mc - 3) * 128, :])
                GP.run(lambda e, dst=dst, sb=sb: e.dma_start(
                    out=dst, in_=sp16[:, sb * 2 * T:(sb + 1) * 2 * T]), inc=sKQ[sb], amt=16)
                sp16_free[sb] = (sKQ[sb], sKQ[sb].n)
            else:
                mv = mc - 2
                if vt_free[mv][0] is not None:
                    VE.wait(*vt_free[mv])
                v_tr = 0
                for blk in range(16):
                    tb = 6 + (blk % 2)
                    pbt = ps[:, tb * 512: tb * 512 + 128]
                    wev(TE, tb)
                    TE.wait(sS, s_sp)
                    t_tr = TE.run(lambda e, sb=sb, blk=blk, pbt=pbt: e.transpose(
                        pbt, spst[:, sb * T + blk * 128: sb * T + (blk + 1) * 128],
                        identity=idf), inc=sT)
                    VE.wait(sT, t_tr)
                    for hh in range(2):
                        VE.run(lambda e, mv=mv, hh=hh, blk=blk, pbt=pbt: e.tensor_copy(
                            vtr2[:, mv * T + hh * 1024 + blk * 64: mv * T + hh * 1024 + (blk + 1) * 64],
                            pbt[:, hh * 64:(hh + 1) * 64]), inc=sV)
                    v_tr = sV.n
                    st8["evac"][tb] = (sV, v_tr)
                GP.wait(sV, v_tr)
                for hh in range(2):
                    hloc = mv * 2 + hh
                    GP.run(lambda e, hg=hg, hloc=hloc, mv=mv, hh=hh: e.dma_start(
                        out=aginv[hg].ap().rearrange("(h j p) f -> h p j f", j=16, p=128)[hloc],
                        in_=vtr2[:, mv * T + hh * 1024: mv * T + (hh + 1) * 1024]
                            .rearrange("p (j f) -> p j f", f=64)), inc=sVD[mv], amt=16)
                vt_free[mv] = (sVD[mv], sVD[mv].n)
                sp_free[sb] = (sT, sT.n)
        wq_rd[wb_] = sT.n
        if hg >= 1:
            GP.wait(sCC, 2 * hg)
        GP.wait(sKQ[0], sKQ[0].n)
        GP.wait(sKQ[1], sKQ[1].n)
        GP.wait(sVD[0], sVD[0].n)
        GP.wait(sVD[1], sVD[1].n)
        GP.run(lambda e, hg=hg: e.collective_compute(
            "AllGather", ALU.bypass, replica_groups=RG,
            ins=[agink[hg].ap().opt()], outs=[kvk[hg].ap().opt()]), inc=sCC)
        GP.run(lambda e, hg=hg: e.collective_compute(
            "AllGather", ALU.bypass, replica_groups=RG,
            ins=[aginv[hg].ap().opt()], outs=[kvv[hg].ap().opt()]), inc=sCC)

    SC.wait(sVD[0], sVD[0].n)
    SC.wait(sVD[1], sVD[1].n)
    VE.wait(sKQ[0], sKQ[0].n)
    VE.wait(sKQ[1], sKQ[1].n)
    VE.wait(sVD[0], sVD[0].n)
    VE.wait(sVD[1], sVD[1].n)
    VE.wait(sT, sT.n)
    for buf in range(2):
        VE.run(lambda e, buf=buf: e.memset(
            vtile[:, buf * 2080:(buf + 1) * 2080].rearrange("p (s f) -> p s f", f=65)[:, :, 64:65], 1.0), inc=sV)
    VE.run(lambda e: e.memset(kthi[64:128, :], 0.0), inc=sV)
    VE.run(lambda e: e.memset(ktlo[64:128, :], 0.0), inc=sV)
    VE.run(lambda e: e.memset(qthi[64:128, :], 0.0), inc=sV)
    VE.run(lambda e: e.memset(qtlo[64:128, :], 0.0), inc=sV)
    VE.run(lambda e: e.memset(rlst, 0.0), inc=sV)
    SY.wait(sKQ[0], sKQ[0].n)
    SY.wait(sKQ[1], sKQ[1].n)
    SY.wait(sVD[0], sVD[0].n)
    SY.wait(sVD[1], sVD[1].n)
    hs_ = {
        "ktile_rd": [wq_rd[0], wq_rd[1]], "qtile_rd": [wq_rd[0], wq_rd[1]], "vtile_rd": [0, 0],
        "in_k": [0, 0], "in_q": [0, 0], "in_v": [0, 0],
        "pt_rd": [0, 0], "exp_done": [0, 0], "score_done": [0, 0],
        "av_evac": [0, 0], "bc_free": 0, "ot_dma": [0, 0], "pending_evac": None,
    }
    TE.wait(sV, sV.n)
    TE.wait(sS, sS.n)

    def plan_prep(h):
        hg, hl, hb = h // 4, h % 4, h % 2
        SY.wait(sCC, 2 * hg + 2)
        SY.wait(sT, hs_["ktile_rd"][hb])
        for sd in range(2):
            for dt, kdst in ((0, kthi), (1, ktlo)):
                SY.run(lambda e, hg=hg, hl=hl, hb=hb, sd=sd, dt=dt, kdst=kdst: e.dma_start(
                    out=kdst[0:64, hb * S + sd * T: hb * S + (sd + 1) * T],
                    in_=kvk[hg].ap()[sd * 256 + hl * 64: sd * 256 + (hl + 1) * 64,
                                     dt * T:(dt + 1) * T]), inc=sK[hb], amt=16)
        hs_["in_k"][hb] = sK[hb].n
        SY.wait(sT, hs_["qtile_rd"][hb])
        for dt, qdst in ((0, qthi), (1, qtlo)):
            SY.run(lambda e, hg=hg, hl=hl, hb=hb, dt=dt, qdst=qdst: e.dma_start(
                out=qdst[0:64, hb * T:(hb + 1) * T],
                in_=qsp[hg].ap()[hl * 64:(hl + 1) * 64, dt * T:(dt + 1) * T]), inc=sQL[hb], amt=16)
        hs_["in_q"][hb] = sQL[hb].n
        SY.wait(sT, hs_["vtile_rd"][hb])
        for sd in range(2):
            SY.run(lambda e, hg=hg, hl=hl, hb=hb, sd=sd: e.dma_start(
                out=vtile[:, hb * 2080:(hb + 1) * 2080].rearrange("p (s f) -> p s f", f=65)[:, sd * 16:(sd + 1) * 16, 0:64],
                in_=kvv[hg].ap()[(sd * 4 + hl) * T: (sd * 4 + hl + 1) * T, :].rearrange("(j p) f -> p j f", p=128)),
                inc=sVL[hb], amt=16)
        hs_["in_v"][hb] = sVL[hb].n

    def plan_head(h):
        hg, hl, hb = h // 4, h % 4, h % 2
        kth = kthi[:, hb * S:(hb + 1) * S]
        ktl = ktlo[:, hb * S:(hb + 1) * S]
        qth = qthi[:, hb * T:(hb + 1) * T]
        qtl = qtlo[:, hb * T:(hb + 1) * T]
        vt = vtile[:, hb * 2080:(hb + 1) * 2080]
        TE.wait(sK[hb], hs_["in_k"][hb])
        TE.wait(sQL[hb], hs_["in_q"][hb])
        TE.wait(sVL[hb], hs_["in_v"][hb])
        for i4 in range(4):
            jmax = 4 * i4 + 3
            chunks = [(sd, j) for sd in range(2) for j in range(jmax + 1)]
            groups = [chunks[i:i + 2] for i in range(0, len(chunks), 2)]
            qsh = qth[:, i4 * TT:(i4 + 1) * TT]
            qsl = qtl[:, i4 * TT:(i4 + 1) * TT]
            ab = (h * 4 + i4) % 2
            av_first = [True]

            def plan_scores(gi, grp):
                bs = gi % 2
                TE.wait(sS, hs_["exp_done"][bs])
                for ci, (sd, j) in enumerate(grp):
                    pb = ps[:, bs * 1024 + ci * 512: bs * 1024 + (ci + 1) * 512]
                    kch = kth[:, (sd * 16 + j) * 128: (sd * 16 + j + 1) * 128]
                    kcl = ktl[:, (sd * 16 + j) * 128: (sd * 16 + j + 1) * 128]
                    TE.run(lambda e, pb=pb, kch=kch, qsh=qsh: e.matmul(pb, kch, qsh, start=True, stop=False))
                    TE.run(lambda e, pb=pb, kch=kch, qsl=qsl: e.matmul(pb, kch, qsl, start=False, stop=False))
                    hs_["score_done"][bs] = TE.run(
                        lambda e, pb=pb, kcl=kcl, qsh=qsh: e.matmul(pb, kcl, qsh, start=False, stop=True),
                        inc=sT)

            def plan_exp(gi, grp):
                bs = gi % 2
                SC.wait(sT, hs_["score_done"][bs])
                SC.wait(sT, hs_["pt_rd"][bs])
                n = len(grp)
                hs_["exp_done"][bs] = SC.run(lambda e, bs=bs, n=n: e.activation(
                    ptile[:, bs * 1024: bs * 1024 + n * 512],
                    ps[:, bs * 1024: bs * 1024 + n * 512], AF.Exp, scale=0.125), inc=sS)
                vmark = 0
                for ci, (sd, j) in enumerate(grp):
                    qb = j - 4 * i4
                    if 0 <= qb <= 3:
                        if vmark == 0:
                            VE.wait(sS, hs_["exp_done"][bs])
                        mk = maskt[:, sd * 128:(sd + 1) * 128]
                        pslice = ptile[:, bs * 1024 + ci * 512 + qb * 128: bs * 1024 + ci * 512 + (qb + 1) * 128]
                        vmark = VE.run(lambda e, pslice=pslice, mk=mk: e.tensor_tensor(
                            pslice, pslice, mk, ALU.mult), inc=sV)
                return vmark

            def plan_av(gi, grp, vmark):
                bs = gi % 2
                TE.wait(sS, hs_["exp_done"][bs])
                if vmark:
                    TE.wait(sV, vmark)
                if av_first[0]:
                    TE.wait(sV, hs_["av_evac"][ab])
                mark = 0
                for ci, (sd, j) in enumerate(grp):
                    startq = 128 * max(0, j - 4 * i4)
                    pl = ptile[:, bs * 1024 + ci * 512 + startq: bs * 1024 + (ci + 1) * 512]
                    vs_ = vt[:, (sd * 16 + j) * 65: (sd * 16 + j) * 65 + 65]
                    st = av_first[0]
                    av_first[0] = False
                    mark = TE.run(lambda e, pl=pl, vs_=vs_, st=st, startq=startq, ab=ab: e.matmul(
                        ps[0:65, (4 + ab) * 512 + startq: (5 + ab) * 512], vs_, pl,
                        start=st, stop=False, skip_group_check=True), inc=sT)
                hs_["pt_rd"][bs] = mark

            vmarks = {}
            pending = None
            for gi, grp in enumerate(groups):
                plan_scores(gi, grp)
                vmarks[gi] = plan_exp(gi, grp)
                if gi == 1 and hs_["pending_evac"] is not None:
                    hs_["pending_evac"]()
                    hs_["pending_evac"] = None
                if pending is not None:
                    plan_av(pending, groups[pending], vmarks[pending])
                pending = gi
            plan_av(pending, groups[pending], vmarks[pending])
            if hs_["pending_evac"] is not None:
                hs_["pending_evac"]()
                hs_["pending_evac"] = None
            t_av = sT.n

            def evac(h=h, i4=i4, ab=ab, t_av=t_av):
                ob = ab
                VE.wait(sT, t_av)
                VE.wait(sOT[ob], hs_["ot_dma"][ob])
                v_rl = VE.run(lambda e, ob=ob, ab=ab: e.reciprocal(rlst[0:1, ob * 512:(ob + 1) * 512],
                                                                   ps[64:65, (4 + ab) * 512:(5 + ab) * 512]), inc=sV)
                TE.wait(sV, v_rl)
                TE.wait(sV, hs_["bc_free"])
                t_bc2 = TE.run(lambda e, ob=ob: e.matmul(ps[:, 3072:3584], bc1,
                                                         rlst[:, ob * 512:(ob + 1) * 512],
                                                         start=True, stop=True), inc=sT)
                VE.run(lambda e, ob=ob, ab=ab: e.tensor_copy(otst[0:64, ob * 512:(ob + 1) * 512],
                                                             ps[0:64, (4 + ab) * 512:(5 + ab) * 512]), inc=sV)
                hs_["av_evac"][ab] = sV.n
                VE.wait(sT, t_bc2)
                v_div = VE.run(lambda e, ob=ob: e.tensor_tensor(
                    otst[0:64, ob * 512:(ob + 1) * 512], otst[0:64, ob * 512:(ob + 1) * 512],
                    ps[0:64, 3072:3584], ALU.mult), inc=sV)
                hs_["bc_free"] = v_div
                GP.wait(sV, v_div)
                GP.run(lambda e, h=h, i4=i4, ob=ob: e.dma_start(
                    out=otd.ap()[h * 64:(h + 1) * 64, i4 * TT:(i4 + 1) * TT],
                    in_=otst[0:64, ob * 512:(ob + 1) * 512]), inc=sOT[ob], amt=16)
                hs_["ot_dma"][ob] = sOT[ob].n

            hs_["pending_evac"] = evac
        hs_["ktile_rd"][hb] = sT.n
        hs_["qtile_rd"][hb] = sT.n
        hs_["vtile_rd"][hb] = sT.n

    plan_prep(0)
    SY.wait(sX1, 16)
    for h in range(NHEAD):
        if h + 1 < NHEAD:
            plan_prep(h + 1)
        if h < C:
            SY.run(lambda e, h=h: e.dma_start(
                out=x1f[:, h * T:(h + 1) * T],
                in_=x1sp.rearrange("(c p) t -> p c t", p=128)[:, h, :]), inc=sX1L, amt=16)
            ld["x1f"] = sX1L.n
        plan_head(h)

    if hs_["pending_evac"] is not None:
        hs_["pending_evac"]()
        hs_["pending_evac"] = None

    v_attn_done = sV.n
    TE.wait(sS, sS.n)
    TE.wait(sV, v_attn_done)
    SY.wait(sV, v_attn_done)
    SY.wait(sX1, 16)
    SY.wait(sT, sT.n)
    SY.wait(sOT[0], sOT[0].n)
    SY.wait(sOT[1], sOT[1].n)
    otf_c = otf.rearrange("p (c t) -> p c t", t=T)
    otd_c = otd.ap().rearrange("(c p) t -> p c t", p=128)
    SY.run(lambda e: e.dma_start(out=otf_c[:, 0:4, :], in_=otd_c[:, 0:4, :]), inc=sOTL, amt=16)
    ld["otf"] = sOTL.n
    GP.wait(sV, v_attn_done)
    GP.wait(sT, sT.n)
    GP.wait(sOT[0], sOT[0].n)
    GP.wait(sOT[1], sOT[1].n)
    GP.run(lambda e: e.dma_start(out=wout_s.rearrange("p (c t) -> p c t", t=D),
                                 in_=wout_d.rearrange("c p t -> p c t")), inc=sGL, amt=16)
    ld["wo"] = sGL.n
    GP.run(lambda e: e.dma_start(out=otf_c[:, 4:8, :], in_=otd_c[:, 4:8, :]), inc=sX2L, amt=16)
    ld["otf2"] = sX2L.n
    TE.wait(sGL, ld["wo"])
    TE.wait(sOTL, ld["otf"])
    for mc in range(C):
        bs = (mc % 2) * 4
        banks = [bs, bs + 1, bs + 2, bs + 3]
        for tt in range(NTT):
            wev(TE, banks[tt])
        for c in range(C):
            if mc == 0 and c == 4:
                TE.wait(sX2L, ld["otf2"])
            wc = wout_s[:, c * D + mc * 128: c * D + (mc + 1) * 128]
            for tt in range(NTT):
                st = (c == 0)
                oc = otf[:, c * T + tt * TT: c * T + tt * TT + TT]
                TE.run(lambda e, wc=wc, oc=oc, st=st, b=banks[tt], lastc=(c == C - 1): e.matmul(
                    ps[:, b * 512:(b + 1) * 512], wc, oc, start=st, stop=lastc),
                    inc=(sT if (c == C - 1 and tt == NTT - 1) else None))
        t_mm = sT.n
        VE.wait(sT, t_mm)
        VE.wait(sX1L, ld["x1f"])
        ob_col = obias[:, mc:mc + 1]
        for tt in range(NTT):
            dst = x1f[:, mc * T + tt * TT: mc * T + tt * TT + TT]
            VE.run(lambda e, b=banks[tt], dst=dst, ob_col=ob_col: e.scalar_tensor_tensor(
                dst, ps[:, b * 512:(b + 1) * 512], ob_col, dst, ALU.add, ALU.add), inc=sV)
            st8["evac"][banks[tt]] = (sV, sV.n)
    v_x2 = sV.n
    if dump == "x2":
        GP.wait(sV, v_x2)
        GP.run(lambda e: e.dma_start(out=x2sp.rearrange("(c p) t -> p c t", p=128),
                                     in_=x1f.rearrange("p (c t) -> p c t", t=T)), inc=sX2, amt=16)
        GP.wait(sX2, 16)
        flush(nc, SY, GP, VE, SC, TE, audit)
        return nc

    t_p4 = sT.n
    v_p4 = sV.n
    VE.wait(sT, t_p4)
    VE.wait(sOT[0], sOT[0].n)
    VE.wait(sOT[1], sOT[1].n)
    SC.wait(sT, t_p4)
    SC.wait(sOT[0], sOT[0].n)
    SC.wait(sOT[1], sOT[1].n)
    SY.wait(sT, t_p4)
    SY.wait(sV, v_p4)
    SY.run(lambda e: e.dma_start(out=qw16, in_=qd.ap()), inc=sQWL, amt=16)
    qw_ld = sQWL.n
    v_ln3, ta_ln3, _ = plan_ln_phase(x1f, None, pre_wait=[(sV, v_p4)])
    TE.wait(sQWL, qw_ld)

    def ffn2_evac(mc, tt, pb, t_mm, bank):
        VE.wait(sT, t_mm)
        dst = x1f[:, mc * T + tt * TT: mc * T + tt * TT + TT]
        VE.run(lambda e, pb=pb, dst=dst: e.scalar_tensor_tensor(dst, pb, HS, dst, ALU.mult, ALU.add), inc=sV)
        st8["evac"][bank] = (sV, sV.n)

    plan_ffn_all(ffn2_evac, v_ln3, ta_ln3)
    GP.wait(sV, sV.n)
    GP.run(lambda e: e.dma_start(out=y_d.rearrange("(c p) t -> p c t", p=128),
                                 in_=x1f.rearrange("p (c t) -> p c t", t=T)), inc=sOUT, amt=16)
    GP.wait(sOUT, 16)
    flush(nc, SY, GP, VE, SC, TE, audit)
    return nc


def flush(nc, SY, GP, VE, SC, TE, audit=False):
    def mk(q, drain_inc=False):
        def body(eng):
            emitted = {}
            for it in q.items:
                if it[0] == "w":
                    eng.wait_ge(it[1], it[2])
                else:
                    ins = it[1](eng)
                    if it[2] is not None:
                        h, cum = it[2]
                        if drain_inc:
                            eng.drain().then_inc(h, cum - emitted.get(id(h), 0))
                            emitted[id(h)] = cum
                        else:
                            ins.then_inc(h, cum - emitted.get(id(h), 0))
                            emitted[id(h)] = cum
                    elif drain_inc:
                        eng.drain()
        return body

    with nc.Block() as block:
        block.sync(mk(SY))
        block.gpsimd(mk(GP))
        block.vector(mk(VE, drain_inc=True))
        block.scalar(mk(SC, drain_inc=True))
        block.tensor(mk(TE))


def prep_inputs(inputs):
    x = np.ascontiguousarray(np.asarray(inputs["L0_attn_x"], dtype=np.float32))
    qkv_w = np.asarray(inputs["L0_attn_qkv_w"], dtype=np.float32)
    qkv_b = np.asarray(inputs["L0_attn_qkv_b"], dtype=np.float32)
    out_w = np.asarray(inputs["L0_attn_out_w"], dtype=np.float32)
    out_b = np.asarray(inputs["L0_attn_out_b"], dtype=np.float32)
    fk = np.asarray(inputs["L0_ffn_kernel"], dtype=np.float32)
    fbias = np.asarray(inputs["L0_ffn_bias"], dtype=np.float32)
    fa = np.asarray(inputs["L0_ffn_a"], dtype=np.float32)
    fb = np.asarray(inputs["L0_ffn_b"], dtype=np.float32)

    qwm, kwm, vwm = qkv_w[:, 0:D], qkv_w[:, D:2 * D], qkv_w[:, 2 * D:3 * D]
    qb_, kb_, vb_ = qkv_b[0:D], qkv_b[D:2 * D], qkv_b[2 * D:3 * D]
    wqkv = np.empty((HG, C, 128, 768), np.float32)
    qkvbc = np.empty((HG, 6, 128), np.float32)
    for g in range(HG):
        cols = np.concatenate([kwm[:, g * 256:(g + 1) * 256],
                               vwm[:, g * 256:(g + 1) * 256],
                               qwm[:, g * 256:(g + 1) * 256]], axis=1)
        wqkv[g] = cols.reshape(C, 128, 768)
        qkvbc[g] = np.concatenate([kb_[g * 256:(g + 1) * 256],
                                   vb_[g * 256:(g + 1) * 256],
                                   qb_[g * 256:(g + 1) * 256]]).reshape(6, 128)
    common = {
        "wqkv": wqkv,
        "qkvbc": np.ascontiguousarray(qkvbc.reshape(HG * 6, 128).T),
        "wout": out_w.reshape(C, 128, D),
        "outbc": np.ascontiguousarray(out_b.reshape(C, 128).T),
        "fker": fk.reshape(C, 128, D),
        "fbias": fbias.reshape(1, D),
        "fa": fa.reshape(C, 128, 16),
        "fb": fb,
        "cones": np.ones((1, T), np.float32),
        "idf": np.eye(128, dtype=np.float32),
    }
    tri = np.triu(np.ones((128, 128), np.float32))
    in_maps = []
    for core in range(8):
        b, p = core // 2, core % 2
        xb = x[b].reshape(32, 128, D)[p::2].reshape(T, D)
        m = np.empty((128, 256), np.float32)
        if p == 0:
            m[:, 0:128] = tri
            m[:, 128:256] = 0.0
        else:
            m[:, 0:128] = 1.0
            m[:, 128:256] = tri
        im = dict(common)
        im["x0"] = np.ascontiguousarray(xb.T)
        im["maskio"] = m
        in_maps.append(im)
    return in_maps


def assemble(results):
    out = np.empty((B, S, D), np.float32)
    for core in range(8):
        b, p = core // 2, core % 2
        yc = results[core]["y"]
        out[b].reshape(32, 128, D)[p::2] = yc.T.reshape(16, 128, D)
    return out


def kernel(**inputs):
    in_maps = prep_inputs(inputs)
    nc = build_nc()
    res = run_bass_kernel_spmd(nc, in_maps, core_ids=list(range(8)))
    return assemble(res.results)
